# version 01
# baseline (speedup 1.0000x reference)
import numpy as np
import ml_dtypes
import concourse.bass as bass
import concourse.mybir as mybir
from concourse.bass_utils import run_bass_kernel_spmd

F32 = mybir.dt.float32
BF16 = mybir.dt.bfloat16
AF = mybir.ActivationFunctionType
ALU = mybir.AluOpType
AX = mybir.AxisListType

H, D, DM, J, T = 32, 64, 2048, 5632, 2048
KT = DM // 128
OT = DM // 128
JT = J // 128
NSUB, SUB = 4, 512
NCH, CH = 2, 1024
NG = SUB // 128
EPS = 1e-5
QEPS = 1e-8
NW = float(J * DM)

ENGINES = ("tensor", "vector", "scalar", "gpsimd", "sync")
ESEM = {"tensor": "t", "vector": "v", "scalar": "c", "gpsimd": "g"}
QSEM = {"sync": "dS", "gpsimd": "dG", "scalar": "dA"}



def eps_order():
    return np.array([2 * (i % 16) + (i // 16) for i in range(32)])


def perm_wv_cols():
    e = eps_order()
    perm = np.empty(DM, dtype=np.int64)
    for d in range(64):
        for i in range(32):
            perm[d * 32 + i] = e[i] * 64 + d
    return perm


def perm_wo_rows():
    e = eps_order()
    perm = np.empty(DM, dtype=np.int64)
    for i1 in range(32):
        for d in range(64):
            perm[i1 * 64 + d] = e[i1] * 64 + d
    return perm


def rope_tables():
    inv = 1.0 / (10000.0 ** (np.arange(0, D, 2) / D))
    t = np.arange(T, dtype=np.float64)
    ang = t[:, None] * inv[None, :]
    cos = np.cos(ang).astype(np.float32)
    sin = np.sin(ang).astype(np.float32)
    cosA = np.empty((16, 128, T), np.float32)
    sinA = np.empty((16, 128, T), np.float32)
    sgn = np.where(np.arange(64) % 2 == 1, -1.0, 1.0).astype(np.float32)
    for a in range(16):
        for half in range(2):
            h = 2 * a + half
            cosA[a, half * 64:(half + 1) * 64, :] = cos[:, h][None, :]
            sinA[a, half * 64:(half + 1) * 64, :] = sgn[:, None] * sin[:, h][None, :]
    return (cosA.astype(ml_dtypes.bfloat16), sinA.astype(ml_dtypes.bfloat16))


def host_prepare(L0_attn_hidden_states, L0_attn_w_q, L0_attn_w_k, L0_attn_w_v,
                 L0_attn_w_o, L0_attn_ln_gamma, L0_attn_ln_beta,
                 L0_ffn_w1, L0_ffn_w2, L0_ffn_w3):
    x = np.asarray(L0_attn_hidden_states, dtype=np.float32)
    wq = np.ascontiguousarray(np.asarray(L0_attn_w_q, np.float32))
    wk = np.ascontiguousarray(np.asarray(L0_attn_w_k, np.float32))
    wv = np.ascontiguousarray(np.asarray(L0_attn_w_v, np.float32)[:, perm_wv_cols()])
    wo = np.ascontiguousarray(np.asarray(L0_attn_w_o, np.float32)[perm_wo_rows(), :])
    gam = np.asarray(L0_attn_ln_gamma, np.float32)
    bet = np.asarray(L0_attn_ln_beta, np.float32)
    gb = np.ascontiguousarray(
        np.concatenate([gam.reshape(KT, 128).T, bet.reshape(KT, 128).T], axis=1))
    w1 = np.asarray(L0_ffn_w1, np.float32)
    w2 = np.asarray(L0_ffn_w2, np.float32)
    w3 = np.asarray(L0_ffn_w3, np.float32)
    w1js = np.ascontiguousarray(w1.T.reshape(DM, JT, 128).transpose(1, 0, 2))
    w2js = np.ascontiguousarray(w2.T.reshape(DM, JT, 128).transpose(1, 0, 2))
    w3js = np.ascontiguousarray(w3.T.reshape(J, OT, 128).transpose(1, 0, 2))
    cosA, sinA = rope_tables()
    shared = dict(
        wq=wq, wk=wk, wv=wv, wo=wo, gb=gb, w1js=w1js, w2js=w2js, w3js=w3js,
        cosA=np.ascontiguousarray(cosA), sinA=np.ascontiguousarray(sinA),
        ident=np.eye(128, dtype=ml_dtypes.bfloat16),
        onesb=np.ones((128, 1), ml_dtypes.bfloat16),
        onesf=np.ones((1, 128), np.float32),
        onescol=np.ones((128, 1), np.float32))
    in_maps = []
    for b in range(8):
        m = dict(shared)
        m["x0T"] = np.ascontiguousarray(x[b].T)
        in_maps.append(m)
    return in_maps



class Sched:
    def __init__(self):
        self.steps = {e: [] for e in ENGINES}
        self.val = {}

    def add(self, eng, emit, waits=(), inc=None, phase=0):
        ws = tuple((s, v) for (s, v) in waits if v > 0)
        self.steps[eng].append((ws, emit, inc, phase))
        if inc is not None:
            s, n = inc
            self.val[s] = self.val.get(s, 0) + n
            return (s, self.val[s])
        return None

    def v(self, sem):
        return self.val.get(sem, 0)

    def emit_phase(self, blk, sems, phase):
        for ename in ENGINES:
            def mk(ename):
                def run(E):
                    for ws, emit, inc, ph in self.steps[ename]:
                        if ph != phase:
                            continue
                        for (s, v) in ws:
                            E.wait_ge(sems[s], v)
                        last = emit(E)
                        if inc is not None:
                            s, n = inc
                            assert last is not None
                            last.then_inc(sems[s], n)
                return run
            getattr(blk, ename)(mk(ename))


NOW = ()



def build_program(debug=False, paranoid=False):
    nc = bass.Bass("TRN2", target_bir_lowering=False)

    x0T = nc.dram_tensor("x0T", [DM, T], F32, kind="ExternalInput").ap()
    wqD = nc.dram_tensor("wq", [DM, DM], F32, kind="ExternalInput").ap()
    wkD = nc.dram_tensor("wk", [DM, DM], F32, kind="ExternalInput").ap()
    wvD = nc.dram_tensor("wv", [DM, DM], F32, kind="ExternalInput").ap()
    woD = nc.dram_tensor("wo", [DM, DM], F32, kind="ExternalInput").ap()
    gbD = nc.dram_tensor("gb", [128, 32], F32, kind="ExternalInput").ap()
    w1js = nc.dram_tensor("w1js", [JT, DM, 128], F32, kind="ExternalInput").ap()
    w2js = nc.dram_tensor("w2js", [JT, DM, 128], F32, kind="ExternalInput").ap()
    w3js = nc.dram_tensor("w3js", [OT, J, 128], F32, kind="ExternalInput").ap()
    cosAD = nc.dram_tensor("cosA", [16, 128, T], BF16, kind="ExternalInput").ap()
    sinAD = nc.dram_tensor("sinA", [16, 128, T], BF16, kind="ExternalInput").ap()
    identD = nc.dram_tensor("ident", [128, 128], BF16, kind="ExternalInput").ap()
    onesbD = nc.dram_tensor("onesb", [128, 1], BF16, kind="ExternalInput").ap()
    onesfD = nc.dram_tensor("onesf", [1, 128], F32, kind="ExternalInput").ap()
    onescD = nc.dram_tensor("onescol", [128, 1], F32, kind="ExternalInput").ap()
    OUT = nc.dram_tensor("out", [DM, T], F32, kind="ExternalOutput").ap()

    xkind = "ExternalOutput" if debug else "Internal"
    x1scr = nc.dram_tensor("dbg_x1" if debug else "x1scr", [DM, T], F32, kind=xkind).ap()
    x1bscr = nc.dram_tensor("x1bscr", [DM, T], BF16, kind="Internal").ap()
    wbfscr = nc.dram_tensor("wbfscr", [4, OT, 128, KT * 128], BF16, kind="Internal").ap()
    w1qscr = nc.dram_tensor("w1qscr", [JT, 128, KT * 128], BF16, kind="Internal").ap()
    w2qscr = nc.dram_tensor("w2qscr", [JT, 128, KT * 128], BF16, kind="Internal").ap()
    w3qscr = nc.dram_tensor("w3qscr", [OT, 128, JT * 128], BF16, kind="Internal").ap()

    S = Sched()

    cursec = ["pre"]

    def sec(name):
        cursec[0] = name

    def _parawaits(phase):
        on = False
        if paranoid is True:
            on = True
        elif paranoid == "A":
            on = phase == 0
        elif paranoid == "B":
            on = phase == 1
        elif isinstance(paranoid, (set, frozenset, list, tuple)):
            on = phase == 0 and cursec[0] in paranoid
        if on:
            return [(x, S.v(x)) for x in ("t", "v", "c", "g", "dS", "dG", "dA")]
        return []

    def dma(eng, out, in_, waits=NOW, phase=0):
        return S.add(eng, (lambda out=out, in_=in_:
                           (lambda E: E.dma_start(out=out, in_=in_)))(),
                     waits=list(waits) + _parawaits(phase),
                     inc=(QSEM[eng], 16), phase=phase)

    def comp(eng, emit, waits=NOW, phase=0):
        return S.add(eng, emit, waits=list(waits) + _parawaits(phase),
                     inc=(ESEM[eng], 1), phase=phase)

    dbg_done = set()
    dbg_arr = {}

    def dbg_arr_slot(name, shape, dtype):
        if name not in dbg_arr:
            dbg_arr[name] = nc.dram_tensor("dbga_" + name, shape, dtype,
                                           kind="ExternalOutput").ap()
        return dbg_arr[name]

    def dbg(name, ap, waits, phase=0):
        if not debug or name in dbg_done:
            return None
        dbg_done.add(name)
        t = nc.dram_tensor("dbgz_" + name, list(ap.shape), ap.dtype,
                           kind="ExternalOutput").ap()
        return dma("sync", t, ap, waits=waits, phase=phase)

    from contextlib import ExitStack
    with ExitStack() as _st:
        en = _st.enter_context
        sb_id = en(nc.sbuf_tensor("sb_id", [128, 128], BF16))
        sb_idf = en(nc.sbuf_tensor("sb_idf", [128, 128], F32))
        sb_onesb = en(nc.sbuf_tensor("sb_onesb", [128, 1], BF16))
        sb_onesf = en(nc.sbuf_tensor("sb_onesf", [1, 128], F32))
        sb_onesc = en(nc.sbuf_tensor("sb_onesc", [128, 1], F32))
        sb_gb = en(nc.sbuf_tensor("sb_gb", [128, 32], F32))
        sb_tq = en(nc.sbuf_tensor("sb_tq", [128, 8], F32))
        sb_gacc = en(nc.sbuf_tensor("sb_gacc", [128, 248], F32))
        sb_grow = en(nc.sbuf_tensor("sb_grow", [1, 16], F32))
        _t = en(nc.semaphore("sem_t")); _v = en(nc.semaphore("sem_v"))
        _c = en(nc.semaphore("sem_c")); _g = en(nc.semaphore("sem_g"))
        _dS = en(nc.semaphore("sem_dS")); _dG = en(nc.semaphore("sem_dG"))
        _dA = en(nc.semaphore("sem_dA"))
        sems = {"t": _t, "v": _v, "c": _c, "g": _g, "dS": _dS, "dG": _dG, "dA": _dA}

        dma("sync", sb_id[:], identD)
        dma("gpsimd", sb_idf[:], identD)
        dma("sync", sb_onesb[:], onesbD)
        dma("sync", sb_onesf[:], onesfD)
        dma("sync", sb_onesc[:], onescD)
        w_const = dma("sync", sb_gb[:], gbD)

        ph_a = {}

        with ExitStack() as _stA:
            enA = _stA.enter_context
            sb_x0b = enA(nc.sbuf_tensor("sb_x0b", [128, KT * SUB], BF16))
            sb_r0 = enA(nc.sbuf_tensor("sb_r0", [128, KT * SUB], BF16))
            sb_r1 = enA(nc.sbuf_tensor("sb_r1", [128, KT * SUB], BF16))
            sb_r2 = enA(nc.sbuf_tensor("sb_r2", [128, KT * SUB], BF16))
            sb_qp = enA(nc.sbuf_tensor("sb_qp", [64, 4 * 128 * 32], BF16))
            sb_kp = enA(nc.sbuf_tensor("sb_kp", [64, 4 * 128 * 32], BF16))
            sb_vp = enA(nc.sbuf_tensor("sb_vp", [32, 64 * 128], BF16))
            sb_p = enA(nc.sbuf_tensor("sb_p", [128, 2 * 512], BF16))
            sb_sm = enA(nc.sbuf_tensor("sb_sm", [128, 2 * 512], BF16))
            sb_pt = enA(nc.sbuf_tensor("sb_pt", [32, 2 * 2048], BF16))
            sb_pt2 = enA(nc.sbuf_tensor("sb_pt2", [128, 2 * 512], BF16))
            sb_negm = enA(nc.sbuf_tensor("sb_negm", [128, 2 * 16], F32))
            sb_sumr = enA(nc.sbuf_tensor("sb_sumr", [128, 2 * 16], F32))
            sb_w = enA(nc.sbuf_tensor("sb_w", [128, 2 * KT * 128], BF16))
            sb_cos = enA(nc.sbuf_tensor("sb_cos", [128, 2 * SUB], BF16))
            sb_sin = enA(nc.sbuf_tensor("sb_sin", [128, 2 * SUB], BF16))
            sb_x0f = enA(nc.sbuf_tensor("sb_x0f", [128, 2 * SUB], F32))
            sb_x1f = enA(nc.sbuf_tensor("sb_x1f", [128, 2 * SUB], F32))
            sb_x1b = enA(nc.sbuf_tensor("sb_x1b", [128, 2 * SUB], BF16))
            sb_row = enA(nc.sbuf_tensor("sb_row", [1, 4 * SUB], F32))
            sb_gw = enA(nc.sbuf_tensor("sb_gw", [128, 2048], F32))
            psA = enA(nc.psum_tensor([128, 1024], F32))
            psB = enA(nc.psum_tensor([128, 1024], F32))
            psC = enA(nc.psum_tensor([128, 1024], F32))
            psX = enA(nc.psum_tensor([128, 512], F32))
            blkA = enA(nc.Block())
            WMATS = [wqD, wkD, wvD, woD]

            gw_free = [NOW, NOW]
            gunit = []
            gcol_last = [0]

            def make_gunit(src, jb, hh, col, red_eng, nrow):
                def unit():
                    slot = 0
                    srcap = src[jb].rearrange("(a p) o -> p a o", p=128)[
                        :, hh * nrow:(hh + 1) * nrow, :]
                    wdone = dma("sync",
                                sb_gw[:, 0: nrow * 128].rearrange(
                                    "p (a o) -> p a o", o=128),
                                srcap,
                                waits=gw_free[slot], phase=0)
                    def f(E, col=col, nrow=nrow):
                        return E.activation(
                            sb_gw[:, 0: nrow * 128],
                            sb_gw[:, 0: nrow * 128],
                            AF.Abs, accum_out=sb_gacc[:, col:col + 1])
                    rdone = comp("scalar", f, waits=[wdone], phase=0)
                    gw_free[slot] = [rdone]
                    gcol_last[0] = col
                    return rdone
                return unit

            col = 0
            gamma_last = [None]
            for src in (w1js, w2js):
                for jb in range(JT):
                    for hh in range(2):
                        gunit.append(make_gunit(src, jb, hh, col, "gpsimd", 8))
                        col += 1
            for ob in range(OT):
                for hh in range(4):
                    gunit.append(make_gunit(w3js, ob, hh, col, "scalar", 11))
                    col += 1
            gtotal = col
            gi = [0]

            def pump_gamma():
                if gi[0] < gtotal:
                    prev = cursec[0]
                    sec("gamma")
                    gamma_last[0] = gunit[gi[0]]()
                    sec(prev)
                    gi[0] += 1

            sq_free = NOW
            w_free = [NOW, NOW]
            cos_free = [NOW, NOW]
            sin_free = [NOW, NOW]
            r0_free = NOW
            r1_free = NOW
            r2_free = NOW
            qp_free = [NOW] * NG
            kp_free = [NOW] * NG
            vp_free = [NOW, NOW]
            psA_free = NOW
            psB_free = [NOW, NOW]
            psC_free = [NOW, NOW]
            p_free = [NOW, NOW]
            sm_free = [NOW, NOW]
            pt_free = [NOW, NOW]
            pt2_free = [NOW, NOW]
            x0f_free = [NOW, NOW]
            x1_free = [NOW, NOW]
            lnb_free = NOW
            last_wo = None

            for sub in range(NSUB):
                t0 = sub * SUB
                sec("pre")
                x0_d = []
                for hh in range(2):
                    x0_d.append(dma(
                        "gpsimd",
                        sb_x0b[:, hh * 8 * SUB:(hh + 1) * 8 * SUB].rearrange(
                            "p (a t) -> p a t", a=8),
                        x0T[hh * 1024:(hh + 1) * 1024, t0:t0 + SUB].rearrange(
                            "(a p) t -> p a t", p=128),
                        waits=list(lnb_free), phase=0))

                sq_done = comp("scalar",
                               lambda E: E.activation(sb_r0[:], sb_x0b[:], AF.Square),
                               waits=x0_d + list(sq_free), phase=0)

                def mk_stats(src, half):
                    def f(E, src=src, half=half):
                        last = None
                        for k in range(KT):
                            last = E.matmul(psA[0:1, half * 512:half * 512 + 512],
                                            sb_onesb[:], src[:, k * 512:(k + 1) * 512],
                                            start=(k == 0), stop=(k == KT - 1))
                        return last
                    return f
                st1 = comp("tensor", mk_stats(sb_x0b, 0),
                           waits=x0_d + list(psA_free), phase=0)
                st2 = comp("tensor", mk_stats(sb_r0, 1), waits=[sq_done], phase=0)

                RM, RR, T1, T2 = 0, SUB, 2 * SUB, 3 * SUB

                def mk_rows1(E):
                    E.tensor_scalar_mul(sb_row[0:1, RM:RM + SUB], psA[0:1, 0:512], 1.0 / DM)
                    E.tensor_scalar_mul(sb_row[0:1, T1:T1 + SUB], psA[0:1, 512:1024], 1.0 / DM)
                    E.engine_nop()
                    E.tensor_mul(sb_row[0:1, T2:T2 + SUB],
                                 sb_row[0:1, RM:RM + SUB], sb_row[0:1, RM:RM + SUB])
                    E.engine_nop()
                    E.tensor_sub(sb_row[0:1, T1:T1 + SUB],
                                 sb_row[0:1, T1:T1 + SUB], sb_row[0:1, T2:T2 + SUB])
                    E.engine_nop()
                    E.tensor_scalar_add(sb_row[0:1, T2:T2 + SUB],
                                        sb_row[0:1, T1:T1 + SUB], EPS)
                    E.engine_nop()
                    E.reciprocal(sb_row[0:1, T2:T2 + SUB], sb_row[0:1, T2:T2 + SUB])
                    E.engine_nop()
                    E.tensor_mul(sb_row[0:1, T1:T1 + SUB],
                                 sb_row[0:1, T1:T1 + SUB], sb_row[0:1, T2:T2 + SUB])
                    E.engine_nop()
                    E.tensor_scalar_add(sb_row[0:1, T1:T1 + SUB],
                                        sb_row[0:1, T1:T1 + SUB], EPS)
                    E.engine_nop()
                    return E.reciprocal(sb_row[0:1, T1:T1 + SUB], sb_row[0:1, T1:T1 + SUB])
                rows1 = comp("vector", mk_rows1, waits=[st1, st2], phase=0)

                def mk_rows2(E):
                    E.activation(sb_row[0:1, T2:T2 + SUB], sb_row[0:1, T2:T2 + SUB], AF.Sqrt)
                    E.nop()
                    return E.activation(sb_row[0:1, T1:T1 + SUB],
                                        sb_row[0:1, T1:T1 + SUB], AF.Sqrt)
                rows2 = comp("scalar", mk_rows2, waits=[rows1], phase=0)

                rows3 = comp("vector",
                             lambda E: E.tensor_mul(sb_row[0:1, RR:RR + SUB],
                                                    sb_row[0:1, T2:T2 + SUB],
                                                    sb_row[0:1, T1:T1 + SUB]),
                             waits=[rows2], phase=0)

                def mk_bc(E):
                    E.matmul(psA[:, 0:512], sb_onesf[:], sb_row[0:1, RM:RM + SUB],
                             start=True, stop=True)
                    return E.matmul(psA[:, 512:1024], sb_onesf[:],
                                    sb_row[0:1, RR:RR + SUB], start=True, stop=True)
                bc_done = comp("tensor", mk_bc, waits=[rows3], phase=0)

                ln_done = None
                for k in range(KT):
                    def mk_ln(E, k=k):
                        sl = slice(k * 512, (k + 1) * 512)
                        E.tensor_sub(sb_sm[:, 0:512], sb_x0b[:, sl], psA[:, 0:512])
                        E.tensor_mul(sb_sm[:, 0:512], sb_sm[:, 0:512], psA[:, 512:1024])
                        return E.tensor_scalar(sb_x0b[:, sl], sb_sm[:, 0:512],
                                               sb_gb[:, k:k + 1], sb_gb[:, 16 + k:17 + k],
                                               ALU.mult, ALU.add)
                    w = [bc_done, st2, w_const] + (list(sm_free[0]) + list(sm_free[1]) +
                                                   list(lnb_free) if k == 0 else [])
                    ln_done = comp("vector", mk_ln, waits=w, phase=0)
                psA_free = [ln_done]
                d_ = dbg("ln1b", sb_x0b[:], [ln_done])
                if d_ is not None:
                    lnb_free = list(lnb_free) + [d_]

                sec("qkv")
                qp_built = [None] * NG
                kp_built = [None] * NG
                vp_built = [None] * NG
                qkv_last_mm = None
                for wi in (0, 1, 2):
                    evac_done = []
                    for a in range(OT):
                        pump_gamma()
                        slot = a % 2
                        if sub == 0:
                            wd1 = dma("gpsimd",
                                      sb_w[:, slot * 2048: slot * 2048 + 1024].rearrange(
                                          "p (a o) -> p a o", o=128),
                                      WMATS[wi][0:1024, a * 128:(a + 1) * 128].rearrange(
                                          "(a p) o -> p a o", p=128),
                                      waits=list(w_free[slot]), phase=0)
                            wd2 = dma("gpsimd",
                                      sb_w[:, slot * 2048 + 1024: slot * 2048 + 2048].rearrange(
                                          "p (a o) -> p a o", o=128),
                                      WMATS[wi][1024:2048, a * 128:(a + 1) * 128].rearrange(
                                          "(a p) o -> p a o", p=128), phase=0)
                            wb = dma("sync", wbfscr[wi, a],
                                     sb_w[:, slot * 2048:(slot + 1) * 2048],
                                     waits=[wd1, wd2], phase=0)
                            wready = [wd1, wd2]
                            wnext_extra = [wb]
                        else:
                            wd = dma("sync", sb_w[:, slot * 2048:(slot + 1) * 2048],
                                     wbfscr[wi, a], waits=list(w_free[slot]), phase=0)
                            wready = [wd]
                            wnext_extra = []
                        if wi in (0, 1):
                            cslot = a % 2
                            cd = dma("scalar", sb_cos[:, cslot * SUB:(cslot + 1) * SUB],
                                     cosAD[a][:, t0:t0 + SUB],
                                     waits=list(cos_free[cslot]), phase=0)
                            sd = dma("scalar", sb_sin[:, cslot * SUB:(cslot + 1) * SUB],
                                     sinAD[a][:, t0:t0 + SUB],
                                     waits=list(sin_free[cslot]), phase=0)
                        pslot = a % 2

                        def mk_mm(E, wslot=slot, pslot=pslot):
                            last = None
                            for k in range(KT):
                                last = E.matmul(
                                    psB[:, pslot * 512:(pslot + 1) * 512],
                                    sb_w[:, wslot * 2048 + k * 128: wslot * 2048 + (k + 1) * 128],
                                    sb_x0b[:, k * 512:(k + 1) * 512],
                                    start=(k == 0), stop=(k == KT - 1))
                            return last
                        mm = comp("tensor", mk_mm,
                                  waits=wready + [ln_done] + list(psB_free[pslot]), phase=0)
                        qkv_last_mm = mm
                        w_free[slot] = [mm] + wnext_extra
                        if wi in (0, 1):
                            def mk_ev(E, a=a, pslot=pslot, cslot=cslot):
                                E.tensor_mul(sb_r0[:, a: 8192: 16],
                                             psB[:, pslot * 512:(pslot + 1) * 512],
                                             sb_cos[:, cslot * SUB:(cslot + 1) * SUB])
                                return E.tensor_mul(sb_r1[:, a: 8192: 16],
                                                    psB[:, pslot * 512:(pslot + 1) * 512],
                                                    sb_sin[:, cslot * SUB:(cslot + 1) * SUB])
                            ev = comp("vector", mk_ev,
                                      waits=[mm, cd, sd] +
                                            (list(r0_free) + list(r1_free) if a == 0 else []),
                                      phase=0)
                            cos_free[cslot] = [ev]
                            sin_free[cslot] = [ev]
                        else:
                            def mk_evv(E, a=a, pslot=pslot):
                                return E.tensor_copy(sb_r0[:, a * 512:(a + 1) * 512],
                                                     psB[:, pslot * 512:(pslot + 1) * 512])
                            ev = comp("vector", mk_evv,
                                      waits=[mm] + (list(r0_free) if a == 0 else []),
                                      phase=0)
                        psB_free[pslot] = [ev]
                        evac_done.append(ev)
                    if wi in (0, 1):
                        pd = []
                        for hp in range(2):
                            pd.append(dma("gpsimd", sb_r2[hp * 64 + 32: hp * 64 + 64, :],
                                          sb_r1[hp * 64: hp * 64 + 64: 2, :],
                                          waits=[evac_done[-1]] +
                                                (list(r2_free) if hp == 0 else []),
                                          phase=0))
                            pd.append(dma("gpsimd", sb_r2[hp * 64: hp * 64 + 32, :],
                                          sb_r1[hp * 64 + 1: hp * 64 + 64: 2, :], phase=0))
                        if wi == 0:
                            d0 = dbg("qc", sb_r0[:], [evac_done[-1]])
                            d1 = dbg("qs", sb_r1[:], [evac_done[-1]])
                            d2 = dbg("qsp", sb_r2[:], [pd[-1]])
                            if d0 is not None:
                                r0_free = list(r0_free) + [d0]
                                r1_free = list(r1_free) + [d1]
                                r2_free = list(r2_free) + [d2]
                        dstp, pfree, built = ((sb_qp, qp_free, qp_built) if wi == 0
                                              else (sb_kp, kp_free, kp_built))
                        for g in range(NG):
                            gslot = g
                            base = gslot * 4096
                            g0 = g * 128

                            def mk_patch(E, base=base, g0=g0, dstp=dstp):
                                last = None
                                for hp in range(2):
                                    src_c = sb_r0[hp * 64:(hp + 1) * 64,
                                                  g0 * 16:(g0 + 128) * 16].rearrange(
                                                      "p (t a) -> p t a", a=16)
                                    src_s = sb_r2[hp * 64:(hp + 1) * 64,
                                                  g0 * 16:(g0 + 128) * 16].rearrange(
                                                      "p (t a) -> p t a", a=16)
                                    dst = dstp[0:64, base:base + 4096].rearrange(
                                        "p (t i) -> p t i", i=32)[:, :, hp * 16:(hp + 1) * 16]
                                    last = E.tensor_add(dst, src_c, src_s)
                                return last
                            bd = comp("vector", mk_patch,
                                      waits=pd + list(pfree[gslot]), phase=0)
                            built[g] = bd
                            if g == 0:
                                d_ = dbg("qp0" if wi == 0 else "kp0",
                                         dstp[0:64, 0:4096], [bd])
                                if d_ is not None:
                                    built[g] = d_
                        r0_free = [built[NG - 1]]
                        r1_free = [pd[-1]]
                        r2_free = [built[NG - 1]]
                    else:
                        v_evac_all = evac_done[-1]
                        d_ = dbg("vsb", sb_r0[:], [v_evac_all])
                        if d_ is not None:
                            v_evac_all = d_
                lnb_free = [qkv_last_mm]

                sec("mid")
                oevac = []
                for g in range(NG):
                    qb = g * 4096
                    vb = 0
                    g0 = g * 128

                    def mk_vp(E, vb=vb, g0=g0):
                        last = None
                        for dl in range(4):
                            src = sb_r0[dl * 32:(dl + 1) * 32, :].rearrange(
                                "p (b t) -> p b t", t=512)[:, :, g0:g0 + 128]
                            dst = sb_vp[0:32, vb:vb + 8192].rearrange(
                                "p (d t) -> p d t", t=128)[:, dl:64:4, :]
                            last = E.tensor_copy(dst, src)
                        return last
                    vp_built[g] = comp("vector", mk_vp,
                                       waits=[v_evac_all] + list(vp_free[0]), phase=0)
                    if g in (0, 3):
                        d_ = dbg("vp0" if g == 0 else "vp3",
                                 sb_vp[0:32, 0:8192], [vp_built[g]])
                        if d_ is not None:
                            vp_built[g] = d_
                    last_pv_for_cslot = {}
                    for bi in range(2):
                        cslot = (g * 2 + bi) % 2

                        def mk_sc(E, qb=qb, bi=bi, cslot=cslot):
                            for tq in range(16):
                                for c in range(4):
                                    tl = bi * 64 + tq * 4 + c
                                    E.matmul(
                                        psC[c * 32:(c + 1) * 32,
                                            cslot * 512 + tq * 32: cslot * 512 + (tq + 1) * 32],
                                        sb_qp[0:64, qb + tl * 32: qb + (tl + 1) * 32],
                                        sb_kp[0:64, qb + tl * 32: qb + (tl + 1) * 32],
                                        start=True, stop=True, tile_position=(0, c * 32))
                            return E.matmul(psX[:], sb_id[:], sb_x0b[:, 0:512],
                                            start=True, stop=True)
                        sc = comp("tensor", mk_sc,
                                  waits=[qp_built[g], kp_built[g]] + list(psC_free[cslot]),
                                  phase=0)
                        qp_free[g] = [sc]
                        kp_free[g] = [sc]

                        sm1a = comp("vector",
                                    (lambda cslot=cslot: lambda E: E.reduce_max(
                                        sb_negm[:, cslot * 16:(cslot + 1) * 16].unsqueeze(2),
                                        psC[:, cslot * 512:(cslot + 1) * 512].rearrange(
                                            "p (g h) -> p g h", h=32),
                                        AX.X, negate=True))(),
                                    waits=[sc], phase=0)

                        def mk_sm1(E, cslot=cslot):
                            ps3 = psC[:, cslot * 512:(cslot + 1) * 512].rearrange(
                                "p (g h) -> p g h", h=32)
                            nm = sb_negm[:, cslot * 16:(cslot + 1) * 16].unsqueeze(2).broadcast_to(
                                [128, 16, 32])
                            sm3 = sb_sm[:, cslot * 512:(cslot + 1) * 512].rearrange(
                                "p (g h) -> p g h", h=32)
                            return E.tensor_add(sm3, ps3, nm)
                        sm1 = comp("vector", mk_sm1,
                                   waits=[sm1a] + list(sm_free[cslot]), phase=0)
                        psC_free[cslot] = [sm1]

                        ex = comp("scalar",
                                  (lambda cslot=cslot:
                                   lambda E: E.activation(
                                       sb_sm[:, cslot * 512:(cslot + 1) * 512],
                                       sb_sm[:, cslot * 512:(cslot + 1) * 512],
                                       AF.Exp, bias=0.0, scale=0.125))(),
                                  waits=[sm1], phase=0)

                        sm2a = comp("vector",
                                    (lambda cslot=cslot: lambda E: E.tensor_reduce(
                                        sb_sumr[:, cslot * 16:(cslot + 1) * 16].unsqueeze(2),
                                        sb_sm[:, cslot * 512:(cslot + 1) * 512].rearrange(
                                            "p (g h) -> p g h", h=32),
                                        AX.X, ALU.add))(),
                                    waits=[ex], phase=0)
                        sm2b = comp("vector",
                                    (lambda cslot=cslot: lambda E: E.reciprocal(
                                        sb_sumr[:, cslot * 16:(cslot + 1) * 16],
                                        sb_sumr[:, cslot * 16:(cslot + 1) * 16]))(),
                                    waits=[sm2a], phase=0)

                        def mk_sm2(E, cslot=cslot):
                            e3 = sb_sm[:, cslot * 512:(cslot + 1) * 512].rearrange(
                                "p (g h) -> p g h", h=32)
                            rc = sb_sumr[:, cslot * 16:(cslot + 1) * 16].unsqueeze(2).broadcast_to(
                                [128, 16, 32])
                            p3 = sb_p[:, cslot * 512:(cslot + 1) * 512].rearrange(
                                "p (g h) -> p g h", h=32)
                            return E.tensor_mul(p3, e3, rc)
                        sm2 = comp("vector", mk_sm2,
                                   waits=[sm2b] + list(p_free[cslot]), phase=0)
                        sm_free[cslot] = [sm2]
                        if debug and sub == 0:
                            pall = dbg_arr_slot("pall", [8, 128, 512], BF16)
                            d_ = dma("sync", pall[g * 2 + bi],
                                     sb_p[:, cslot * 512:(cslot + 1) * 512],
                                     waits=[sm2], phase=0)
                            sm2 = d_

                        ptt = comp("vector",
                                   (lambda cslot=cslot: lambda E: E.transpose(
                                       sb_pt2[:, cslot * 512:(cslot + 1) * 512],
                                       sb_p[:, cslot * 512:(cslot + 1) * 512]))(),
                                   waits=[sm2] + list(pt2_free[cslot]), phase=0)

                        def mk_ptc(E, cslot=cslot):
                            last = None
                            for c in range(4):
                                last = E.copy(
                                    sb_pt[0:32, cslot * 2048 + c * 512:
                                          cslot * 2048 + (c + 1) * 512],
                                    sb_pt2[c * 32:(c + 1) * 32,
                                           cslot * 512:(cslot + 1) * 512])
                            return last
                        ptd = comp("scalar", mk_ptc,
                                   waits=[ptt] + list(pt_free[cslot]), phase=0)
                        pt2_free[cslot] = [ptd]
                        ptdone = [ptd]
                        p_free[cslot] = [ptt]
                        if debug and sub == 0:
                            ptall = dbg_arr_slot("ptall", [8, 32, 2048], BF16)
                            d_ = dma("sync", ptall[g * 2 + bi],
                                     sb_pt[0:32, cslot * 2048:(cslot + 1) * 2048],
                                     waits=[ptd], phase=0)
                            ptdone[-1] = d_

                        for ob in range(2):
                            obidx = g * 4 + bi * 2 + ob
                            eslot = obidx % 2

                            def mk_pv(E, bi=bi, ob=ob, vb=vb, cslot=cslot, eslot=eslot):
                                last = None
                                for w in range(32):
                                    tl_b = ob * 32 + w
                                    tl_g = bi * 64 + tl_b
                                    tq, c = tl_b // 4, tl_b % 4
                                    c2, sl16 = w % 2, w // 2
                                    for dh in range(2):
                                        colp = c2 * 64 + dh * 32
                                        last = E.matmul(
                                            psB[colp:colp + 32,
                                                eslot * 512 + sl16 * 32:
                                                eslot * 512 + (sl16 + 1) * 32],
                                            sb_vp[0:32,
                                                  vb + dh * 4096 + tl_g:
                                                  vb + (dh + 1) * 4096: 128],
                                            sb_pt[0:32, cslot * 2048 + c * 512 + tq * 32:
                                                  cslot * 2048 + c * 512 + (tq + 1) * 32],
                                            start=True, stop=True,
                                            tile_position=(0, colp))
                                return E.matmul(psX[:], sb_id[:], sb_x0b[:, 0:512],
                                                start=True, stop=True)
                            pv = comp("tensor", mk_pv,
                                      waits=[ptdone[-1], vp_built[g]] + list(psB_free[eslot]),
                                      phase=0)
                            last_pv_for_cslot[cslot] = pv
                            vp_free[0] = [pv]

                            def mk_oe(E, obidx=obidx, eslot=eslot):
                                return E.tensor_copy(sb_r2[:, obidx * 512:(obidx + 1) * 512],
                                                     psB[:, eslot * 512:(eslot + 1) * 512])
                            oe = comp("vector", mk_oe,
                                      waits=[pv] + (list(r2_free) if obidx == 0 else []),
                                      phase=0)
                            psB_free[eslot] = [oe]
                            oevac.append(oe)
                        pt_free[cslot] = [last_pv_for_cslot[cslot]]

                r0_free = [vp_built[NG - 1]]
                sq_free = [vp_built[NG - 1], qkv_last_mm]
                d_ = dbg("otsb", sb_r2[:], [oevac[-1]])
                if d_ is not None:
                    oevac = oevac + [d_]
                    r2_free_extra = [d_]
                else:
                    r2_free_extra = []

                ob_done = []
                for c2 in range(2):
                    for hp in range(2):
                        def mk_ob(E, c2=c2, hp=hp):
                            src = sb_r2[c2 * 64:(c2 + 1) * 64, :].rearrange(
                                "p (b s i) -> p b s i", s=16, i=32)[:, :, :, hp:32:2]
                            src = src.transpose([0, 3, 1, 2])
                            dst = sb_r1[hp * 64:(hp + 1) * 64, :].rearrange(
                                "p (a b w) -> p a b w", b=16, w=32)[:, :, :, c2:32:2]
                            return E.tensor_copy(dst, src)
                        ob_done.append(comp("vector", mk_ob,
                                            waits=[oevac[-1]] +
                                                  (list(r1_free) if (c2 == 0 and hp == 0)
                                                   else []),
                                            phase=0))
                r2_free = [ob_done[-1]] + r2_free_extra
                d_ = dbg("ot", sb_r1[:], [ob_done[-1]])
                if d_ is not None:
                    ob_done = ob_done + [d_]

                sec("wo")
                psA_slot_free = [psA_free, psA_free]
                for a in range(OT):
                    pump_gamma()
                    slot = a % 2
                    if sub == 0:
                        wd1 = dma("gpsimd",
                                  sb_w[:, slot * 2048: slot * 2048 + 1024].rearrange(
                                      "p (a o) -> p a o", o=128),
                                  woD[0:1024, a * 128:(a + 1) * 128].rearrange(
                                      "(a p) o -> p a o", p=128),
                                  waits=list(w_free[slot]), phase=0)
                        wd2 = dma("gpsimd",
                                  sb_w[:, slot * 2048 + 1024: slot * 2048 + 2048].rearrange(
                                      "p (a o) -> p a o", o=128),
                                  woD[1024:2048, a * 128:(a + 1) * 128].rearrange(
                                      "(a p) o -> p a o", p=128), phase=0)
                        wb = dma("sync", wbfscr[3, a],
                                 sb_w[:, slot * 2048:(slot + 1) * 2048],
                                 waits=[wd1, wd2], phase=0)
                        wready = [wd1, wd2]
                        wnext_extra = [wb]
                    else:
                        wd = dma("sync", sb_w[:, slot * 2048:(slot + 1) * 2048],
                                 wbfscr[3, a], waits=list(w_free[slot]), phase=0)
                        wready = [wd]
                        wnext_extra = []
                    pslot = a % 2

                    def mk_womm(E, wslot=slot, pslot=pslot):
                        last = None
                        for k in range(KT):
                            last = E.matmul(
                                psA[:, pslot * 512:(pslot + 1) * 512],
                                sb_w[:, wslot * 2048 + k * 128: wslot * 2048 + (k + 1) * 128],
                                sb_r1[:, k * 512:(k + 1) * 512],
                                start=(k == 0), stop=(k == KT - 1))
                        return last
                    womm = comp("tensor", mk_womm,
                                waits=wready + [ob_done[-1]] + list(psA_slot_free[pslot]),
                                phase=0)
                    w_free[slot] = [womm] + wnext_extra
                    last_wo = womm
                    xslot = a % 2
                    xd = dma("sync", sb_x0f[:, xslot * SUB:(xslot + 1) * SUB],
                             x0T[a * 128:(a + 1) * 128, t0:t0 + SUB],
                             waits=list(x0f_free[xslot]), phase=0)

                    def mk_res(E, pslot=pslot, xslot=xslot):
                        E.tensor_add(sb_x1f[:, xslot * SUB:(xslot + 1) * SUB],
                                     psA[:, pslot * 512:(pslot + 1) * 512],
                                     sb_x0f[:, xslot * SUB:(xslot + 1) * SUB])
                        return E.tensor_copy(sb_x1b[:, xslot * SUB:(xslot + 1) * SUB],
                                             sb_x1f[:, xslot * SUB:(xslot + 1) * SUB])
                    res = comp("vector", mk_res,
                               waits=[womm, xd] + list(x1_free[xslot]), phase=0)
                    psA_slot_free[pslot] = [res]
                    x0f_free[xslot] = [res]
                    xs1 = dma("scalar", x1scr[a * 128:(a + 1) * 128, t0:t0 + SUB],
                              sb_x1f[:, xslot * SUB:(xslot + 1) * SUB],
                              waits=[res], phase=0)
                    xs2 = dma("scalar", x1bscr[a * 128:(a + 1) * 128, t0:t0 + SUB],
                              sb_x1b[:, xslot * SUB:(xslot + 1) * SUB], phase=0)
                    x1_free[xslot] = [xs1, xs2]
                psA_free = psA_slot_free[0] + psA_slot_free[1]
                r1_free = [last_wo]

            ph_a["x1_done"] = x1_free[0] + x1_free[1] + [last_wo]
            ph_a["gamma_done"] = [gamma_last[0],
                                  ("g", S.v("g")), ("c", S.v("c"))]
            assert gi[0] == gtotal, f"gamma units not fully pumped: {gi[0]}/{gtotal}"

            S.emit_phase(blkA, sems, 0)

        with ExitStack() as _stB:
            enB = _stB.enter_context
            sb_ln2 = enB(nc.sbuf_tensor("sb_ln2", [128, KT * CH], BF16))
            sb_h = enB(nc.sbuf_tensor("sb_h", [128, JT * CH], BF16))
            sb_wq1 = enB(nc.sbuf_tensor("sb_wq1", [128, 2 * 2048], BF16))
            sb_wq2 = enB(nc.sbuf_tensor("sb_wq2", [128, 2 * 2048], BF16))
            sb_wraw = enB(nc.sbuf_tensor("sb_wraw", [128, 2048], F32))
            sb_sA1 = enB(nc.sbuf_tensor("sb_sA1", [128, 2048], BF16))
            sb_w3q = enB(nc.sbuf_tensor("sb_w3q", [128, 2 * J], BF16))
            sb_x1bt = enB(nc.sbuf_tensor("sb_x1bt", [128, 2 * CH], BF16))
            sb_sq2 = enB(nc.sbuf_tensor("sb_sq2", [128, CH], BF16))
            sb_tmp2 = enB(nc.sbuf_tensor("sb_tmp2", [128, CH], BF16))
            sb_sg = enB(nc.sbuf_tensor("sb_sg", [128, CH], BF16))
            sb_x1ft = enB(nc.sbuf_tensor("sb_x1ft", [128, CH], F32))
            sb_outT = enB(nc.sbuf_tensor("sb_outT", [128, CH], F32))
            sb_row2 = enB(nc.sbuf_tensor("sb_row2", [1, 3 * CH], F32))
            pG = enB(nc.psum_tensor([128, 1024], F32))
            pH = enB(nc.psum_tensor([128, 1024], F32))
            pI = enB(nc.psum_tensor([128, 1024], F32))
            pJ = enB(nc.psum_tensor([128, 1024], F32))
            blkB = enB(nc.Block())
            def mk_gfin(E):
                E.tensor_reduce(sb_gacc[:, 240:241], sb_gacc[:, 0:88], AX.X, ALU.add)
                E.tensor_reduce(sb_gacc[:, 241:242], sb_gacc[:, 88:176], AX.X, ALU.add)
                return E.tensor_reduce(sb_gacc[:, 242:243], sb_gacc[:, 176:240],
                                       AX.X, ALU.add)
            gfin = comp("vector", mk_gfin, waits=ph_a["gamma_done"], phase=1)
            gmm3 = comp("tensor",
                        lambda E: E.matmul(pI[0:1, 0:3], sb_onesc[:],
                                           sb_gacc[:, 240:243], start=True, stop=True),
                        waits=[gfin], phase=1)

            def mk_trow(E):
                return E.tensor_scalar(sb_grow[0:1, 0:3], pI[0:1, 0:3],
                                       0.5 / NW, 0.5 * QEPS, ALU.mult, ALU.add)
            trow = comp("vector", mk_trow, waits=[gmm3], phase=1)
            tq_done = []
            prev_tq = None
            for i in range(3):
                mm2 = comp("tensor",
                           (lambda i=i: lambda E: E.matmul(
                               pI[0:128, 0:1], sb_onesf[:], sb_grow[0:1, i:i + 1],
                               start=True, stop=True))(),
                           waits=[trow] + ([prev_tq] if prev_tq else []), phase=1)

                def mk_tq(E, i=i):
                    E.tensor_scalar_mul(sb_tq[:, 2 * i:2 * i + 1], pI[0:128, 0:1], -1.0)
                    return E.tensor_copy(sb_tq[:, 2 * i + 1:2 * i + 2], pI[0:128, 0:1])
                tqd = comp("vector", mk_tq, waits=[mm2], phase=1)
                tq_done.append(tqd)
                prev_tq = tqd
            pI_free = [tq_done[-1]]
            pJ_free = NOW

            x1bt_free = [NOW, NOW]
            wraw_free = NOW
            wq1_free = [NOW, NOW]
            wq2_free = [NOW, NOW]
            w3q_free = [NOW, NOW]
            sA1_free = NOW
            pG_free = NOW
            pH_free = NOW
            x1f_free = NOW
            out_free = NOW
            sg_free = NOW
            sq2_free = NOW
            tmp2_free = NOW
            ln2_free = NOW
            h_free = NOW
            pIJ_free = [pI_free, pJ_free]
            final_out = None

            for ch in range(NCH):
                c0 = ch * CH
                stat_last = None
                for k in range(KT):
                    slot = k % 2
                    xw = list(x1bt_free[slot])
                    if ch == 0 and k == 0:
                        xw += ph_a["x1_done"]
                    xd = dma("sync", sb_x1bt[:, slot * CH:(slot + 1) * CH],
                             x1bscr[k * 128:(k + 1) * 128, c0:c0 + CH],
                             waits=xw, phase=1)
                    sqd = comp("scalar",
                               (lambda slot=slot: lambda E: E.activation(
                                   sb_sq2[:], sb_x1bt[:, slot * CH:(slot + 1) * CH],
                                   AF.Square))(),
                               waits=[xd] + list(sq2_free), phase=1)

                    def mk_st(E, k=k, slot=slot):
                        last = None
                        for hh in range(2):
                            last = E.matmul(pG[0:1, hh * 512:(hh + 1) * 512], sb_onesb[:],
                                            sb_x1bt[:, slot * CH + hh * 512:
                                                    slot * CH + (hh + 1) * 512],
                                            start=(k == 0), stop=(k == KT - 1))
                        for hh in range(2):
                            last = E.matmul(pH[0:1, hh * 512:(hh + 1) * 512], sb_onesb[:],
                                            sb_sq2[:, hh * 512:(hh + 1) * 512],
                                            start=(k == 0), stop=(k == KT - 1))
                        return last
                    std = comp("tensor", mk_st,
                               waits=[sqd] + (list(pG_free) + list(pH_free) if k == 0 else []),
                               phase=1)
                    sq2_free = [std]
                    x1bt_free[slot] = [std]
                    stat_last = std

                RM, RR, RT = 0, CH, 2 * CH

                def mk_r21(E):
                    E.tensor_scalar_mul(sb_row2[0:1, RM:RM + CH], pG[0:1, 0:1024], 1.0 / DM)
                    E.tensor_scalar_mul(sb_row2[0:1, RT:RT + CH], pH[0:1, 0:1024], 1.0 / DM)
                    E.engine_nop()
                    E.tensor_mul(sb_row2[0:1, RR:RR + CH],
                                 sb_row2[0:1, RM:RM + CH], sb_row2[0:1, RM:RM + CH])
                    E.engine_nop()
                    E.tensor_sub(sb_row2[0:1, RT:RT + CH],
                                 sb_row2[0:1, RT:RT + CH], sb_row2[0:1, RR:RR + CH])
                    E.engine_nop()
                    E.tensor_scalar_add(sb_row2[0:1, RT:RT + CH],
                                        sb_row2[0:1, RT:RT + CH], EPS)
                    E.engine_nop()
                    return E.reciprocal(sb_row2[0:1, RT:RT + CH], sb_row2[0:1, RT:RT + CH])
                r21 = comp("vector", mk_r21, waits=[stat_last], phase=1)
                r22 = comp("scalar",
                           lambda E: E.activation(sb_row2[0:1, RR:RR + CH],
                                                  sb_row2[0:1, RT:RT + CH], AF.Sqrt),
                           waits=[r21], phase=1)

                def mk_bc2(E):
                    last = None
                    for hh in range(2):
                        E.matmul(pG[:, hh * 512:(hh + 1) * 512], sb_onesf[:],
                                 sb_row2[0:1, RM + hh * 512:RM + (hh + 1) * 512],
                                 start=True, stop=True)
                        last = E.matmul(pH[:, hh * 512:(hh + 1) * 512], sb_onesf[:],
                                        sb_row2[0:1, RR + hh * 512:RR + (hh + 1) * 512],
                                        start=True, stop=True)
                    return last
                bc2 = comp("tensor", mk_bc2, waits=[r22, ("v", S.v("v"))], phase=1)

                ln2_done = None
                for k in range(KT):
                    slot = k % 2
                    xd = dma("sync", sb_x1bt[:, slot * CH:(slot + 1) * CH],
                             x1bscr[k * 128:(k + 1) * 128, c0:c0 + CH],
                             waits=list(x1bt_free[slot]), phase=1)

                    def mk_ln2(E, k=k, slot=slot):
                        E.tensor_sub(sb_tmp2[:], sb_x1bt[:, slot * CH:(slot + 1) * CH],
                                     pG[:, 0:1024])
                        return E.tensor_mul(sb_ln2[:, k * CH:(k + 1) * CH],
                                            sb_tmp2[:], pH[:, 0:1024])
                    ln2_done = comp("vector", mk_ln2,
                                    waits=[xd, bc2] + list(tmp2_free) +
                                          (list(ln2_free) if k == 0 else []), phase=1)
                    tmp2_free = [ln2_done]
                    x1bt_free[slot] = [ln2_done]
                pG_free = [ln2_done]
                pH_free = [ln2_done]

                h_last = None
                for j in range(JT):
                    slot = j % 2
                    wq_ready = []
                    for wi, (scr, srcw, dstq, qfree) in enumerate((
                            (w1qscr, w1js, sb_wq1, wq1_free),
                            (w2qscr, w2js, sb_wq2, wq2_free))):
                        if ch == 0:
                            rd = []
                            for hh in range(2):
                                rd.append(dma(
                                    "gpsimd",
                                    sb_wraw[:, hh * 1024:(hh + 1) * 1024].rearrange(
                                        "p (a o) -> p a o", o=128),
                                    srcw[j][hh * 1024:(hh + 1) * 1024, :].rearrange(
                                        "(a p) o -> p a o", p=128),
                                    waits=(list(wraw_free) if hh == 0 else []), phase=1))

                            def mk_sgn(E, wi=wi, slot=slot, dstq=dstq):
                                E.activation(sb_sA1[:], sb_wraw[:], AF.Sign,
                                             bias=sb_tq[:, 2 * wi:2 * wi + 1])
                                return E.activation(dstq[:, slot * 2048:(slot + 1) * 2048],
                                                    sb_wraw[:], AF.Sign,
                                                    bias=sb_tq[:, 2 * wi + 1:2 * wi + 2])
                            sgn = comp("scalar", mk_sgn,
                                       waits=rd + [tq_done[wi]] + list(sA1_free) +
                                             list(qfree[slot]), phase=1)
                            wraw_free = [sgn]

                            def mk_cmb(E, slot=slot, dstq=dstq):
                                return E.tensor_add(dstq[:, slot * 2048:(slot + 1) * 2048],
                                                    sb_sA1[:],
                                                    dstq[:, slot * 2048:(slot + 1) * 2048])
                            cmb = comp("vector", mk_cmb, waits=[sgn], phase=1)
                            sA1_free = [cmb]
                            dma("sync", scr[j], dstq[:, slot * 2048:(slot + 1) * 2048],
                                waits=[cmb], phase=1)
                            wq_ready.append(cmb)
                        else:
                            wqd = dma("sync", dstq[:, slot * 2048:(slot + 1) * 2048],
                                      scr[j], waits=list(qfree[slot]), phase=1)
                            wq_ready.append(wqd)

                    def mk_x12(E, slot=slot):
                        last = None
                        for k in range(KT):
                            for hh in range(2):
                                last = E.matmul(
                                    pG[:, hh * 512:(hh + 1) * 512],
                                    sb_wq1[:, slot * 2048 + k * 128: slot * 2048 + (k + 1) * 128],
                                    sb_ln2[:, k * CH + hh * 512: k * CH + (hh + 1) * 512],
                                    start=(k == 0), stop=(k == KT - 1))
                        for k in range(KT):
                            for hh in range(2):
                                last = E.matmul(
                                    pH[:, hh * 512:(hh + 1) * 512],
                                    sb_wq2[:, slot * 2048 + k * 128: slot * 2048 + (k + 1) * 128],
                                    sb_ln2[:, k * CH + hh * 512: k * CH + (hh + 1) * 512],
                                    start=(k == 0), stop=(k == KT - 1))
                        return last
                    x12 = comp("tensor", mk_x12,
                               waits=wq_ready + [ln2_done] + list(pG_free) + list(pH_free),
                               phase=1)
                    wq1_free[slot] = [x12]
                    wq2_free[slot] = [x12]

                    sig = comp("scalar",
                               lambda E: E.activation(sb_sg[:], pG[:, 0:1024], AF.Sigmoid,
                                                      bias=0.0, scale=0.5),
                               waits=[x12] + list(sg_free), phase=1)
                    pG_free = [sig]

                    def mk_h(E, j=j):
                        return E.scalar_tensor_tensor(sb_h[:, j * CH:(j + 1) * CH],
                                                      sb_sg[:], 0.5, pH[:, 0:1024],
                                                      ALU.mult, ALU.mult)
                    hd = comp("vector", mk_h,
                              waits=[sig] + (list(h_free) if j == 0 else []), phase=1)
                    sg_free = [hd]
                    pH_free = [hd]
                    h_last = hd

                for ob in range(OT):
                    slot = ob % 2
                    if ch == 0:
                        w3_ready = []
                        for qu in range(4):
                            rd = dma("gpsimd",
                                     sb_wraw[:, 0:1408].rearrange("p (a o) -> p a o", o=128),
                                     w3js[ob][qu * 1408:(qu + 1) * 1408, :].rearrange(
                                         "(a p) o -> p a o", p=128),
                                     waits=list(wraw_free), phase=1)

                            def mk_s3(E, slot=slot, qu=qu):
                                E.activation(sb_sA1[:, 0:1408], sb_wraw[:, 0:1408],
                                             AF.Sign, bias=sb_tq[:, 4:5])
                                return E.activation(
                                    sb_w3q[:, slot * J + qu * 1408: slot * J + (qu + 1) * 1408],
                                    sb_wraw[:, 0:1408], AF.Sign, bias=sb_tq[:, 5:6])
                            s3 = comp("scalar", mk_s3,
                                      waits=[rd, tq_done[2]] + list(sA1_free) +
                                            (list(w3q_free[slot]) if qu == 0 else []),
                                      phase=1)
                            wraw_free = [s3]

                            def mk_c3(E, slot=slot, qu=qu):
                                return E.tensor_add(
                                    sb_w3q[:, slot * J + qu * 1408: slot * J + (qu + 1) * 1408],
                                    sb_sA1[:, 0:1408],
                                    sb_w3q[:, slot * J + qu * 1408: slot * J + (qu + 1) * 1408])
                            c3 = comp("vector", mk_c3, waits=[s3], phase=1)
                            sA1_free = [c3]
                            w3_ready.append(c3)
                        dma("sync", w3qscr[ob], sb_w3q[:, slot * J:(slot + 1) * J],
                            waits=w3_ready, phase=1)
                    else:
                        w3rd = dma("sync", sb_w3q[:, slot * J:(slot + 1) * J],
                                   w3qscr[ob], waits=list(w3q_free[slot]), phase=1)
                        w3_ready = [w3rd]

                    pslot = ob % 2
                    pps = pI if pslot == 0 else pJ

                    def mk_w3mm(E, slot=slot, pps=pps):
                        last = None
                        for jt in range(JT):
                            for hh in range(2):
                                last = E.matmul(
                                    pps[:, hh * 512:(hh + 1) * 512],
                                    sb_w3q[:, slot * J + jt * 128: slot * J + (jt + 1) * 128],
                                    sb_h[:, jt * CH + hh * 512: jt * CH + (hh + 1) * 512],
                                    start=(jt == 0), stop=(jt == JT - 1))
                        return last
                    w3mm = comp("tensor", mk_w3mm,
                                waits=w3_ready + [h_last] + list(pIJ_free[pslot]), phase=1)
                    w3q_free[slot] = [w3mm]
                    if ob == OT - 1:
                        h_free = [w3mm]
                        ln2_free = [w3mm]

                    xw = list(x1f_free)
                    if ch == 0 and ob == 0:
                        xw += ph_a["x1_done"]
                    xd = dma("sync", sb_x1ft[:], x1scr[ob * 128:(ob + 1) * 128, c0:c0 + CH],
                             waits=xw, phase=1)

                    def mk_out(E, pps=pps):
                        return E.scalar_tensor_tensor(sb_outT[:], pps[:, 0:1024], 0.5,
                                                      sb_x1ft[:], ALU.mult, ALU.add)
                    od = comp("vector", mk_out,
                              waits=[w3mm, xd] + list(out_free), phase=1)
                    pIJ_free[pslot] = [od]
                    x1f_free = [od]
                    final_out = dma("sync", OUT[ob * 128:(ob + 1) * 128, c0:c0 + CH],
                                    sb_outT[:], waits=[od], phase=1)
                    out_free = [final_out]

            S.add("sync", lambda E: None, waits=[("dS", S.v("dS"))], phase=1)
            S.add("gpsimd", lambda E: None, waits=[("dG", S.v("dG"))], phase=1)
            S.add("scalar", lambda E: None, waits=[("dA", S.v("dA"))], phase=1)
            S.add("vector", lambda E: None, waits=[("dS", S.v("dS"))], phase=1)
            S.add("tensor", lambda E: None, waits=[("dS", S.v("dS"))], phase=1)

            S.emit_phase(blkB, sems, 1)

    return nc


def kernel(**inputs):
    in_maps = host_prepare(**inputs)
    nc = build_program(paranoid="A")
    res = run_bass_kernel_spmd(nc, in_maps, core_ids=list(range(8)))
    outs = [np.asarray(r["out"]) for r in res.results]
    return np.stack([o.T for o in outs], axis=0).astype(np.float32)
